# version 01
# baseline (speedup 1.0000x reference)
import contextlib
import numpy as np
import ml_dtypes
import concourse.bass as bass
import concourse.mybir as mybir
from concourse.alu_op_type import AluOpType as ALU
from concourse.bass_utils import run_bass_kernel_spmd

F32 = mybir.dt.float32
BF16 = mybir.dt.bfloat16
AF = mybir.ActivationFunctionType
AX = mybir.AxisListType.X
BF = ml_dtypes.bfloat16

D = 2048
KC = 16
EPS_LN = 1e-5
EPS_ATTN = 1e-6
NCORE = 8
ENGS = ("t", "v", "s", "g", "y")


class B:

    def __init__(self, name="prog"):
        self.nc = bass.Bass("TRN2", target_bir_lowering=False)
        self.es = contextlib.ExitStack()
        self.q = {e: [] for e in ENGS}
        self.sems = {}
        self.cnt = {}
        for e in ENGS:
            self._mksem(e + "c")
        self.NS = 2
        for q in ("gd", "yd", "sd"):
            for i in range(self.NS):
                self._mksem(q + str(i))
        self.dma_ord = {"gd": 0, "yd": 0, "sd": 0}
        self.watermark = {e: {} for e in ENGS}

    def _mksem(self, nm):
        self.sems[nm] = self.es.enter_context(self.nc.semaphore(nm))
        self.cnt[nm] = 0

    def sb(self, name, shape, dtype):
        return self.es.enter_context(self.nc.sbuf_tensor(name, shape, dtype))

    def ps(self, name, shape, dtype=F32):
        return self.es.enter_context(self.nc.psum_tensor(name, shape, dtype))

    def din(self, name, shape, dtype=F32):
        return self.nc.dram_tensor(name, list(shape), dtype, kind="ExternalInput").ap()

    def dout(self, name, shape, dtype=F32):
        return self.nc.dram_tensor(name, list(shape), dtype, kind="ExternalOutput").ap()

    def op(self, eng, fn, deps=()):
        ch = eng + "c"
        self.cnt[ch] += 1
        tok = (ch, self.cnt[ch])
        deps = [d for d in deps if d is not None]

        def thunk(e, _deps=tuple(deps), _fn=fn, _tok=tok):
            wm = self.watermark[eng]
            for (s, val) in _deps:
                if eng + "c" == s:
                    pass
                if wm.get(s, -1) < val:
                    e.wait_ge(self.sems[s], val)
                    wm[s] = val
            instr = _fn(e)
            instr.then_inc(self.sems[_tok[0]], 1)
            wm[_tok[0]] = max(wm.get(_tok[0], -1), 0)

        self.q[eng].append(thunk)
        return tok

    def dma(self, eng, fn, deps=()):
        q = {"g": "gd", "y": "yd", "s": "sd"}[eng]
        i = self.dma_ord[q]
        self.dma_ord[q] = i + 1
        ch = q + str(i % self.NS)
        self.cnt[ch] += 16
        tok = (ch, self.cnt[ch])
        deps = [d for d in deps if d is not None]

        def thunk(e, _deps=tuple(deps), _fn=fn, _tok=tok):
            wm = self.watermark[eng]
            if _tok[1] > 16 and wm.get(_tok[0], -1) < _tok[1] - 16:
                e.wait_ge(self.sems[_tok[0]], _tok[1] - 16)
                wm[_tok[0]] = _tok[1] - 16
            for (s, val) in _deps:
                if wm.get(s, -1) < val:
                    e.wait_ge(self.sems[s], val)
                    wm[s] = val
            instr = _fn(e)
            instr.then_inc(self.sems[_tok[0]], 16)

        self.q[eng].append(thunk)
        return tok

    def barrier_wait(self, eng, toks):
        def thunk(e, _deps=tuple(t for t in toks if t is not None)):
            wm = self.watermark[eng]
            for (s, val) in _deps:
                if wm.get(s, -1) < val:
                    e.wait_ge(self.sems[s], val)
                    wm[s] = val
        self.q[eng].append(thunk)

    def finalize(self):
        for q, eng in (("gd", "g"), ("yd", "y"), ("sd", "s")):
            toks = [(q + str(i), self.cnt[q + str(i)]) for i in range(self.NS)
                    if self.cnt[q + str(i)] > 0]
            if toks:
                self.barrier_wait(eng, toks)
        nc = self.nc
        with nc.Block() as blk:
            for eng, dec in (("t", blk.tensor), ("v", blk.vector), ("s", blk.scalar),
                             ("g", blk.gpsimd), ("y", blk.sync)):
                thunks = self.q[eng]
                if not thunks:
                    continue

                def section(e, _thunks=tuple(thunks)):
                    for th in _thunks:
                        th(e)
                dec(section)
        self.es.close()
        return nc


def emit_consts(b):
    c = {}
    c["ones"] = b.sb("c_ones", [128, 1], BF16)
    c["onesd"] = b.sb("c_onesd", [128, 1], BF16)
    c["ones128i"] = b.sb("c_ones128i", [128, 1], BF16)
    c["negones"] = b.sb("c_negones", [128, 1], BF16)
    c["onesr"] = b.sb("c_onesr", [1, 128], BF16)
    c["ident"] = b.sb("c_ident", [128, 128], BF16)
    rowi = b.sb("c_rowi", [128, 128], mybir.dt.int32)
    coli = b.sb("c_coli", [128, 128], mybir.dt.int32)
    identf = b.sb("c_identf", [128, 128], F32)
    c["identf"] = identf

    def f(g):
        g.memset(c["ones"][:], 1.0)
        g.memset(c["onesd"][:], 1.0 / 2048.0)
        g.memset(c["ones128i"][:], 1.0 / 128.0)
        g.memset(c["negones"][:], -1.0)
        g.memset(c["onesr"][:], 1.0)
        g.iota(rowi[:], pattern=[[0, 128]], channel_multiplier=1)
        return g.iota(coli[:], pattern=[[1, 128]], channel_multiplier=0)
    t0 = b.op("g", f)
    c["rowi"] = rowi
    c["coli"] = coli
    c["iota_tok"] = t0
    t1 = b.op("v", lambda v: v.tensor_tensor(out=identf[:], in0=rowi[:], in1=coli[:],
                                             op=ALU.is_equal), deps=[t0])
    c["done"] = b.op("v", lambda v: v.tensor_copy(out=c["ident"][:], in_=identf[:]),
                     deps=[t1])
    return c


def emit_ln_stats(b, c0, xti, sqt, pstat, pstat2, pbc, ptr, rows, rowsf, murow,
                  rstdrow, rstdbc, rstdtm, dep_x, scrtok, TILE=512):
    nsub = TILE // 128
    def fsum(t):
        for k in range(KC):
            tt = t.matmul(pstat[0:1, 0:TILE], c0["onesd"][:, 0:1],
                          xti[:, k * TILE:(k + 1) * TILE],
                          start=(k == 0), stop=(k == KC - 1))
        return tt
    t_mu = b.op("t", fsum, deps=[dep_x, scrtok.get("pstat_free"), scrtok.get("pstat_free2")])
    t_e2 = None
    for r in range(4):
        def fsqr(v, _r=r):
            for j in range(4):
                tt = v.tensor_tensor(out=sqt[:, j * TILE:(j + 1) * TILE],
                                     in0=xti[:, (_r * 4 + j) * TILE:(_r * 4 + j + 1) * TILE],
                                     in1=xti[:, (_r * 4 + j) * TILE:(_r * 4 + j + 1) * TILE],
                                     op=ALU.mult)
            return tt
        t_sqr = b.op("v", fsqr, deps=[dep_x, t_e2])

        def fsum2(t, _r=r):
            for j in range(4):
                tt = t.matmul(pstat2[0:1, 0:TILE], c0["onesd"][:, 0:1],
                              sqt[:, j * TILE:(j + 1) * TILE],
                              start=(_r == 0 and j == 0), stop=(_r == 3 and j == 3))
            return tt
        t_e2 = b.op("t", fsum2,
                    deps=[t_sqr] + ([scrtok.get("pstat2_free")] if r == 0 else []))

    t_mu2 = b.op("s", lambda s: s.activation(out=rows[0:1, 0:TILE],
                                             in_=pstat[0:1, 0:TILE], func=AF.Square),
                 deps=[t_mu, t_e2])
    t_var = b.op("v", lambda v: v.tensor_tensor(out=rows[0:1, TILE:2 * TILE],
                                                in0=pstat2[0:1, 0:TILE],
                                                in1=rows[0:1, 0:TILE], op=ALU.subtract),
                 deps=[t_e2, t_mu2])
    t_sd = b.op("s", lambda s: s.activation(out=rows[0:1, 2 * TILE:3 * TILE],
                                            in_=rows[0:1, TILE:2 * TILE], func=AF.Sqrt,
                                            bias=b.epsln[0:1, 0:1], scale=1.0),
                deps=[t_var])
    t_rsf = b.op("v", lambda v: v.reciprocal(out=rowsf[0:1, 0:TILE],
                                             in_=rows[0:1, 2 * TILE:3 * TILE]),
                 deps=[t_sd])
    t_rs = b.op("v", lambda v: v.tensor_copy(out=rstdrow[0:1, 0:TILE],
                                             in_=rowsf[0:1, 0:TILE]), deps=[t_rsf])
    t_mur = b.op("v", lambda v: v.tensor_copy(out=murow[0:1, 0:TILE],
                                              in_=pstat[0:1, 0:TILE]), deps=[t_mu, t_e2])

    def fbc(t):
        return t.matmul(pbc[:, 0:TILE], c0["onesr"][0:1, :], rstdrow[0:1, 0:TILE],
                        start=True, stop=True)
    t_bc = b.op("t", fbc, deps=[t_rs, scrtok.get("pbc_free")])

    def ftr(t):
        for s_ in range(nsub):
            tt = t.transpose(ptr[:, s_:s_ + 1],
                             rowsf[0:1, s_ * 128:(s_ + 1) * 128],
                             c0["identf"][0:1, 0:1])
        return tt
    t_tr = b.op("t", ftr, deps=[t_rsf, scrtok.get("ptr_free")])
    t_bcc = b.op("v", lambda v: v.tensor_copy(out=rstdbc[:, 0:TILE], in_=pbc[:, 0:TILE]),
                 deps=[t_bc])
    t_tmc = b.op("v", lambda v: v.tensor_copy(out=rstdtm[:, 0:nsub], in_=ptr[:, 0:nsub]),
                 deps=[t_tr])
    scrtok["pstat_free"] = t_mur
    scrtok["pstat_free2"] = t_mu2
    scrtok["pstat2_free"] = t_var
    scrtok["pbc_free"] = t_bcc
    scrtok["ptr_free"] = t_tmc
    return {"mu": t_mur, "rstd": t_rs, "bc": t_bcc, "tm": t_tmc, "var": t_var}


def emit_ffn_tile(b, c0, W, xti, sqt, scr, resid_tile, out_tile, out_scale_half,
                  TILE=512):
    ln = emit_ln_stats(b, c0, xti, sqt, W["pstat"], W["pstat2"], W["pbc"], W["ptr"],
                       scr["rows"], scr["rowsf"], scr["murow"], scr["rstdrow"],
                       scr["rstdbc"], scr["rstdtm"], dep_x=scr["dep_x"], scrtok=scr,
                       TILE=TILE)
    nsub = TILE // 128
    ptm, pgf, pmid, pr = W["ptm"], W["pgf"], W["pmid"], W["pr"]

    def flog(t):
        for s_ in range(nsub):
            for k in range(KC):
                t.matmul(ptm[:, s_ * 8:(s_ + 1) * 8],
                         xti[:, k * TILE + s_ * 128:k * TILE + (s_ + 1) * 128],
                         W["rw"][:, k * 8:(k + 1) * 8], start=(k == 0), stop=False)
            tt = t.matmul(ptm[:, s_ * 8:(s_ + 1) * 8],
                          scr["murow"][0:1, s_ * 128:(s_ + 1) * 128],
                          W["nrs"][0:1, 0:8], start=False, stop=True)
        return tt
    t_log = b.op("t", flog, deps=[scr["dep_x"], ln["mu"],
                                  scr.get("ptm_free", W.get("ptm_init_free"))])

    lg = scr["logits"]
    d3 = lambda ap: ap.rearrange("p (s e) -> p s e", e=8)
    rstdtm_bc = scr["rstdtm"][:, 0:nsub].unsqueeze(2).to_broadcast((128, nsub, 8))
    t1 = b.op("v", lambda v: v.tensor_tensor(out=d3(lg[:, 0:nsub * 8]),
                                             in0=d3(ptm[:, 0:nsub * 8]),
                                             in1=rstdtm_bc, op=ALU.mult),
              deps=[t_log, ln["tm"]])
    rb_bc = W["rbrep"][:, 0:8].unsqueeze(1).to_broadcast((128, nsub, 8))
    t2 = b.op("v", lambda v: v.tensor_tensor(out=d3(lg[:, 0:nsub * 8]),
                                             in0=d3(lg[:, 0:nsub * 8]),
                                             in1=rb_bc, op=ALU.add), deps=[t1])
    lg3 = lg[:, 0:nsub * 8].rearrange("p (s e) -> p s e", e=8)
    m1 = scr["m1c"]
    m2 = scr["m2c"]
    t3 = b.op("v", lambda v: v.reduce_max(out=m1[:, 0:nsub], in_=lg3, axis=AX), deps=[t2])
    m1bc = m1[:, 0:nsub].unsqueeze(2).to_broadcast((128, nsub, 8))
    msk1 = scr["gmask"]
    t4 = b.op("v", lambda v: v.tensor_tensor(out=d3(msk1[:, 0:nsub * 8]),
                                             in0=lg3, in1=m1bc,
                                             op=ALU.is_equal), deps=[t3])
    lm = scr["lgt2"]
    t5 = b.op("v", lambda v: v.scalar_tensor_tensor(out=lm[:, 0:nsub * 8],
                                                    in0=msk1[:, 0:nsub * 8],
                                                    scalar=-1e30, in1=lg[:, 0:nsub * 8],
                                                    op0=ALU.mult, op1=ALU.add),
              deps=[t4])
    lm3 = lm[:, 0:nsub * 8].rearrange("p (s e) -> p s e", e=8)
    t6 = b.op("v", lambda v: v.reduce_max(out=m2[:, 0:nsub], in_=lm3, axis=AX), deps=[t5])
    m2bc = m2[:, 0:nsub].unsqueeze(2).to_broadcast((128, nsub, 8))
    msk2 = scr["gmask2"]
    t7 = b.op("v", lambda v: v.tensor_tensor(out=d3(msk2[:, 0:nsub * 8]),
                                             in0=lm3, in1=m2bc,
                                             op=ALU.is_equal), deps=[t6])
    dd = scr["ddc"]
    t8 = b.op("v", lambda v: v.tensor_tensor(out=dd[:, 0:nsub], in0=m1[:, 0:nsub],
                                             in1=m2[:, 0:nsub], op=ALU.subtract),
              deps=[t6])
    sg = scr["sg1"]
    t9 = b.op("s", lambda s: s.activation(out=sg[:, 0:nsub], in_=dd[:, 0:nsub],
                                          func=AF.Sigmoid), deps=[t8])
    dm = scr["lgt2"]
    t10 = b.op("v", lambda v: v.tensor_tensor(out=dm[:, 0:nsub * 8],
                                              in0=msk1[:, 0:nsub * 8],
                                              in1=msk2[:, 0:nsub * 8], op=ALU.subtract),
               deps=[t7])
    sgbc = sg[:, 0:nsub].unsqueeze(2).to_broadcast((128, nsub, 8))
    t11 = b.op("v", lambda v: v.tensor_tensor(out=d3(dm[:, 0:nsub * 8]),
                                              in0=d3(dm[:, 0:nsub * 8]), in1=sgbc,
                                              op=ALU.mult), deps=[t9, t10])
    gate = scr["gate"]
    t12 = b.op("v", lambda v: v.tensor_tensor(out=gate[:, 0:nsub * 8],
                                              in0=dm[:, 0:nsub * 8],
                                              in1=msk2[:, 0:nsub * 8], op=ALU.add),
               deps=[t11])
    if "dbg" in scr:
        scr["dbg"]["gate"] = (gate, t12)
        scr["dbg"]["murow"] = (scr["murow"], ln["mu"])
        scr["dbg"]["rstdrow"] = (scr["rstdrow"], ln["rstd"])
        scr["dbg"]["logits"] = (lg, t2)
        scr["dbg"]["m1c"] = (m1, t3)
    gr = scr["gaterep"]
    t_gfm = []
    for s_ in range(nsub):
        gbc = gate[:, s_ * 8:(s_ + 1) * 8].unsqueeze(2).to_broadcast((128, 8, 16))
        tg1 = b.op("v", lambda v, _g=gbc: v.tensor_single_scalar(
            out=gr[:, 0:128].rearrange("p (e r) -> p e r", r=16), in_=_g,
            scalar=2.0, op=ALU.mult),
            deps=[t12, scr.get("grep_free")])
        tg2 = b.op("t", lambda t, _s=s_: t.transpose(pgf[:, _s * 128:(_s + 1) * 128],
                                                     gr[:, 0:128], c0["ident"][:]),
                   deps=[tg1, scr.get("pgf_free") if s_ == 0 else None])
        scr["grep_free"] = tg2
        t_gfm.append(tg2)
    t_gfmc = b.op("v", lambda v: v.tensor_copy(out=scr["gfm"][:, 0:TILE],
                                               in_=pgf[:, 0:TILE]), deps=t_gfm)
    scr["pgf_free"] = t_gfmc

    def fmid(t):
        for k in range(KC):
            t.matmul(pmid[:, 0:TILE], W["la"][:, k * 128:(k + 1) * 128],
                     xti[:, k * TILE:(k + 1) * TILE], start=(k == 0), stop=False)
        return t.matmul(pmid[:, 0:TILE], W["nas"][0:1, 0:128], scr["murow"][0:1, 0:TILE],
                        start=False, stop=True)
    t_mid = b.op("t", fmid, deps=[scr["dep_x"], ln["mu"], scr.get("pmid_free")])
    t_midg = b.op("v", lambda v: v.tensor_tensor(out=scr["midg"][:, 0:TILE],
                                                 in0=pmid[:, 0:TILE],
                                                 in1=scr["gfm"][:, 0:TILE], op=ALU.mult),
                  deps=[t_mid, t_gfmc])
    if "dbg" in scr:
        scr["dbg"]["gfm"] = (scr["gfm"], t_gfmc)
        scr["dbg"]["midg"] = (scr["midg"], t_midg)
    scr["pmid_free"] = t_midg
    scr["ptm_free"] = t1

    out_toks = []
    ev_prev = [scr.get("pr0_free"), scr.get("pr1_free")]
    for fc in range(KC):
        def fres(t, _fc=fc):
            for k in range(KC):
                t.matmul(pr[_fc % 2][:, 0:TILE],
                         W["bw"][:, k * D + _fc * 128:k * D + _fc * 128 + 128],
                         xti[:, k * TILE:(k + 1) * TILE], start=(k == 0), stop=False)
            t.matmul(pr[_fc % 2][:, 0:TILE], W["lb"][:, _fc * 128:(_fc + 1) * 128],
                     scr["midg"][:, 0:TILE], start=False, stop=False)
            return t.matmul(pr[_fc % 2][:, 0:TILE],
                            W["nbs"][0:1, _fc * 128:(_fc + 1) * 128],
                            scr["murow"][0:1, 0:TILE], start=False, stop=True)
        t_mm = b.op("t", fres, deps=[t_midg, ev_prev[fc % 2]])

        def fev1(v, _fc=fc):
            return v.tensor_tensor(out=scr["ev1"][:, 0:TILE], in0=pr[_fc % 2][:, 0:TILE],
                                   in1=scr["rstdbc"][:, 0:TILE], op=ALU.mult)
        te1 = b.op("v", fev1, deps=[t_mm, ln["bc"], scr.get("ev1_free")])
        ev_prev[fc % 2] = te1

        def fev2(v, _fc=fc):
            return v.tensor_scalar(out=scr["ev1"][:, 0:TILE], in0=scr["ev1"][:, 0:TILE],
                                   scalar1=W["bb"][:, _fc:_fc + 1],
                                   scalar2=b.halfcol[:, 0:1],
                                   op0=ALU.add, op1=ALU.mult)
        te2 = b.op("v", fev2, deps=[te1])

        def fev3(v, _fc=fc):
            return v.tensor_tensor(out=out_tile[:, _fc * TILE:(_fc + 1) * TILE],
                                   in0=scr["ev1"][:, 0:TILE],
                                   in1=resid_tile[:, _fc * TILE:(_fc + 1) * TILE],
                                   op=ALU.add)
        te3 = b.op("v", fev3, deps=[te2, scr.get("out_free")])
        scr["ev1_free"] = te3
        out_toks.append(te3)
    scr["pr0_free"] = ev_prev[0]
    scr["pr1_free"] = ev_prev[1]
    return out_toks


def ffn_psums(b):
    pstt = b.ps("pstt", [128, 512])
    return {
        "pstt": pstt,
        "pstat": pstt[0:1, 0:512],
        "pstat2": pstt[32:33, 0:512],
        "pbc": b.ps("pbc", [128, 512]),
        "ptm": b.ps("ptm", [128, 32]),
        "ptr": b.ps("ptr", [128, 4]),
        "pgf": b.ps("pgf", [128, 512], BF16),
        "pmid": b.ps("pmid", [128, 512]),
        "pr": [b.ps("pr0", [128, 512]), b.ps("pr1", [128, 512])],
    }


def ffn_scratch(b, TILE=512):
    return {
        "rows": b.sb("rows", [1, 3 * TILE], F32),
        "rowsf": b.sb("rowsf", [1, TILE], F32),
        "murow": b.sb("murow", [1, TILE], BF16),
        "rstdrow": b.sb("rstdrow", [1, TILE], BF16),
        "rstdbc": b.sb("rstdbc", [128, TILE], F32),
        "rstdtm": b.sb("rstdtm", [128, 4], F32),
        "logits": b.sb("slogits", [128, 32], F32),
        "lgt2": b.sb("slgt2", [128, 32], F32),
        "m1c": b.sb("sm1c", [128, 4], F32),
        "m2c": b.sb("sm2c", [128, 4], F32),
        "ddc": b.sb("sddc", [128, 4], F32),
        "sg1": b.sb("ssg1", [128, 4], F32),
        "gmask": b.sb("sgmask", [128, 32], F32),
        "gmask2": b.sb("sgmask2", [128, 32], F32),
        "gate": b.sb("sgate", [128, 32], F32),
        "gaterep": b.sb("sgaterep", [128, 128], BF16),
        "gfm": b.sb("sgfm", [128, TILE], BF16),
        "midg": b.sb("smidg", [128, TILE], BF16),
        "ev1": b.sb("sev1", [128, TILE], F32),
    }


def emit_ffn_weight_load_prep(b, c0, W, hb):
    toks = []
    for dst, src in hb:
        toks.append(b.dma("g", lambda g, _d=dst, _s=src: g.dma_start(out=_d, in_=_s)))

    pstat = W["pstat"]
    prev = None
    for o in range(4):
        def fsum(t, _o=o):
            for k in range(KC):
                tt = t.matmul(pstat[0:1, 0:512], c0["negones"][:, 0:1],
                              W["bw"][:, k * D + _o * 512:k * D + (_o + 1) * 512],
                              start=(k == 0), stop=(k == KC - 1))
            return tt
        tm = b.op("t", fsum, deps=toks + [prev])
        prev = b.op("v", lambda v, _o=o: v.tensor_copy(
            out=W["nbs"][0:1, _o * 512:(_o + 1) * 512], in_=pstat[0:1, 0:512]), deps=[tm])

    def frs(t):
        for k in range(KC):
            tt = t.matmul(pstat[0:1, 0:8], c0["negones"][:, 0:1],
                          W["rw"][:, k * 8:(k + 1) * 8], start=(k == 0), stop=(k == KC - 1))
        return tt
    tm = b.op("t", frs, deps=toks + [prev])
    t_nrs = b.op("v", lambda v: v.tensor_copy(out=W["nrs"][0:1, 0:8],
                                              in_=pstat[0:1, 0:8]), deps=[tm])

    def fas(t):
        for k in range(KC):
            tt = t.matmul(pstat[0:1, 128:256], c0["negones"][:, 0:1],
                          W["la"][:, k * 128:(k + 1) * 128],
                          start=(k == 0), stop=(k == KC - 1))
        return tt
    tm2 = b.op("t", fas, deps=toks + [t_nrs])
    t_nas = b.op("v", lambda v: v.tensor_copy(out=W["nas"][0:1, 0:128],
                                              in_=pstat[0:1, 128:256]), deps=[tm2])

    t_rb16 = b.op("v", lambda v: v.tensor_copy(out=W["rbrow"][0:1, 0:8],
                                               in_=W["rbr"][0:1, 0:8]),
                  deps=[W["rb_ld"]])
    t_rbb = b.op("t", lambda t: t.matmul(W["ptm"][:, 0:8], c0["onesr"][0:1, :],
                                         W["rbrow"][0:1, 0:8], start=True, stop=True),
                 deps=[t_rb16, t_nas])
    t_rbc = b.op("v", lambda v: v.tensor_copy(out=W["rbrep"][:, 0:8],
                                              in_=W["ptm"][:, 0:8]), deps=[t_rbb])
    W["ptm_init_free"] = t_rbc
    return [prev, t_nrs, t_nas, t_rbc]


def build_ffn_launch(ntok, with_wo, in_f32, out_f32, debug=False, TILE=512):
    b = B()
    ntile = ntok // TILE

    xin = b.din("xin", [KC, 128, ntok], F32 if in_f32 else BF16)
    bwT = b.din("bwT", [D, D], F32)
    rwT = b.din("rwT", [D, 8], F32)
    lafh = b.din("laf", [D, 128], F32)
    lbfh = b.din("lbf", [128, D], F32)
    bbt = b.din("bbt", [128, KC], F32)
    rbv = b.din("rbv", [1, 8], F32)
    if with_wo:
        ogh = b.din("og", [KC, 128, ntok], BF16)
        woh = b.din("wo", [D, D], F32)
    xout = b.dout("xout", [KC, 128, ntok], F32 if out_f32 else BF16)

    c0 = emit_consts(b)
    b.epsln = b.sb("epsln", [1, 1], F32)
    b.halfcol = b.sb("halfcol", [128, 1], F32)
    b.op("g", lambda g: (g.memset(b.epsln[:], EPS_LN), g.memset(b.halfcol[:], 0.5))[-1])

    W = ffn_psums(b)
    W["bw"] = b.sb("bw", [128, KC * D], BF16)
    W["rw"] = b.sb("rw", [128, KC * 8], BF16)
    W["la"] = b.sb("la", [128, KC * 128], BF16)
    W["lb"] = b.sb("lb", [128, D], BF16)
    W["bb"] = b.sb("bb", [128, KC], F32)
    W["rbr"] = b.sb("rbr", [1, 8], F32)
    W["rbrow"] = b.sb("rbrow", [1, 8], BF16)
    W["rbrep"] = b.sb("rbrep", [128, 8], F32)
    W["nbs"] = b.sb("nbs", [1, D], BF16)
    W["nrs"] = b.sb("nrs", [1, 8], BF16)
    W["nas"] = b.sb("nas", [1, 128], BF16)
    W["rb_ld"] = b.dma("y", lambda y: y.dma_start(out=W["rbr"][0:1, 0:8], in_=rbv))

    hb = [
        (W["bw"][:].rearrange("p (k o) -> p k o", k=KC),
         bwT.rearrange("(k p) o -> p k o", p=128)),
        (W["rw"][:].rearrange("p (k o) -> p k o", k=KC),
         rwT.rearrange("(k p) o -> p k o", p=128)),
        (W["la"][:].rearrange("p (k o) -> p k o", k=KC),
         lafh.rearrange("(k p) o -> p k o", p=128)),
        (W["lb"][:], lbfh),
        (W["bb"][:], bbt),
    ]
    if with_wo:
        wos = b.sb("wos", [128, KC * D], BF16)
        hb.append((wos[:].rearrange("p (k o) -> p k o", k=KC),
                   woh.rearrange("(k p) o -> p k o", p=128)))
    wprep = emit_ffn_weight_load_prep(b, c0, W, hb)

    scr = ffn_scratch(b, TILE)
    xt = [b.sb(f"xt{i}", [128, KC * TILE], BF16) for i in range(2)]
    sqt = b.sb("sqt", [128, 4 * TILE], BF16)
    if with_wo:
        outt0 = b.sb("outt0", [128, KC * TILE], BF16)
        outt = [outt0, outt0]
    else:
        outt = [b.sb(f"outt{i}", [128, KC * TILE], BF16) for i in range(2)]
    if with_wo:
        ot0 = b.sb("ogt0", [128, KC * TILE], BF16)
        ot = [ot0, ot0]
        x2t = [b.sb(f"x2t{i}", [128, KC * TILE], BF16) for i in range(2)]

    free_tok = [None, None]
    store_toks = []
    last_store = None
    for i in range(ntile):
        pp = i % 2
        t_ld = b.dma("g", lambda g, _i=i, _p=pp: g.dma_start(
            out=xt[_p][:].rearrange("p (k t) -> p k t", k=KC),
            in_=xin[:, :, _i * TILE:(_i + 1) * TILE]
            .rearrange("k p t -> p k t")), deps=[free_tok[pp]] + (wprep if i < 2 else []))
        if with_wo:
            t_og = b.dma("g", lambda g, _i=i, _p=pp: g.dma_start(
                out=ot[_p][:].rearrange("p (k t) -> p k t", k=KC),
                in_=ogh[:, :, _i * TILE:(_i + 1) * TILE]
                .rearrange("k p t -> p k t")),
                deps=[free_tok[pp], scr.get("ot_free")])
            x2_toks = []
            mm_prev = [scr.get("pr0_free"), scr.get("pr1_free")]
            for fc in range(KC):
                def fwo(t, _fc=fc, _p=pp):
                    for k in range(KC):
                        tt = t.matmul(W["pr"][_fc % 2][:, 0:TILE],
                                      wos[:, k * D + _fc * 128:k * D + _fc * 128 + 128],
                                      ot[_p][:, k * TILE:(k + 1) * TILE],
                                      start=(k == 0), stop=(k == KC - 1))
                    return tt
                t_mm = b.op("t", fwo, deps=[t_og, mm_prev[fc % 2]] + (wprep if i == 0 and fc == 0 else []))

                def fx2(v, _fc=fc, _p=pp):
                    return v.tensor_tensor(out=x2t[_p][:, _fc * TILE:(_fc + 1) * TILE],
                                           in0=W["pr"][_fc % 2][:, 0:TILE],
                                           in1=xt[_p][:, _fc * TILE:(_fc + 1) * TILE],
                                           op=ALU.add)
                t_x2 = b.op("v", fx2, deps=[t_mm, t_ld])
                mm_prev[fc % 2] = t_x2
                x2_toks.append(t_x2)
            scr["pr0_free"] = mm_prev[0]
            scr["pr1_free"] = mm_prev[1]
            scr["ot_free"] = x2_toks[-1]
            scr["dep_x"] = x2_toks[-1]
            xti = x2t[pp]
            resid = x2t[pp]
        else:
            scr["dep_x"] = t_ld
            xti = xt[pp]
            resid = xt[pp]

        if i == 0 and debug:
            scr["dbg"] = {}
        scr["out_free"] = last_store if with_wo else free_tok[pp]
        toks = emit_ffn_tile(b, c0, W, xti[:], sqt[:], scr, resid[:], outt[pp][:],
                             True, TILE)
        if i == 0 and debug:
            dbg = scr.pop("dbg")
            for nm, (tile, tok) in dbg.items():
                shp = list(tile.shape) if hasattr(tile, "shape") else None
                ap = tile[:] if hasattr(tile, "ap") else tile
                dr = b.dout("dbg_" + nm, list(ap.shape), ap.dtype)
                b.dma("y", lambda y, _d=dr, _a=ap: y.dma_start(out=_d, in_=_a),
                      deps=[tok])
        st_eng = "g" if out_f32 else "y"
        t_st = b.dma(st_eng, lambda e, _i=i, _p=pp: e.dma_start(
            out=xout[:, :, _i * TILE:(_i + 1) * TILE].rearrange("k p t -> p k t"),
            in_=outt[_p][:].rearrange("p (k t) -> p k t", k=KC)), deps=toks)
        store_toks.append(t_st)
        free_tok[pp] = t_st
        last_store = t_st

    b.barrier_wait("y", store_toks[-2:])
    return b.finalize()








def scan_state(b, H=8):
    st = {}
    st["S_sb"] = b.sb("sc_Ssb", [128, H * 128], BF16)
    st["S_f32"] = b.sb("sc_Sf", [128, H * 128], F32)
    st["GT"] = b.sb("sc_GT", [128, H * 128], BF16)
    st["AQT"] = b.sb("sc_AQT", [128, H * 128], BF16)
    st["keitm"] = b.sb("sc_kei", [128, H * 128], BF16)
    st["vntm"] = b.sb("sc_vntm", [128, H * 128], BF16)
    st["vnfm"] = b.sb("sc_vnfm", [128, H * 128], BF16)
    st["khb"] = b.sb("sc_khb", [128, 128], BF16)
    st["sqb"] = b.sb("sc_sqb", [128, 128], BF16)
    st["gwb"] = b.sb("sc_gwb", [128, 128], BF16)
    st["gkb"] = b.sb("sc_gkb", [128, 128], BF16)
    st["khat"] = b.sb("sc_khat", [128, 128], F32)
    st["f1"] = b.sb("sc_f1", [128, 128], F32)
    st["f2"] = b.sb("sc_f2", [128, 128], F32)
    st["f3"] = b.sb("sc_f3", [128, 128], F32)
    st["rstd"] = b.sb("sc_rstd", [128, 128], F32)
    st["murd"] = b.sb("sc_murd", [128, 128], F32)
    st["tb"] = b.sb("sc_tb", [128, 128], BF16)
    st["t2"] = b.sb("sc_t2", [128, 128], BF16)
    st["eta_tm"] = b.sb("sc_eta", [128, 8], F32)
    st["neta"] = b.sb("sc_neta", [128, 8], F32)
    st["blockmask"] = b.sb("sc_bm", [128, 128], BF16)
    st["w2"] = b.sb("sc_w2", [128, 8], F32)
    st["ones128"] = b.sb("sc_ones128", [128, 128], BF16)
    st["epsa"] = b.sb("sc_epsa", [128, 1], F32)
    st["PK"] = [b.ps("sc_pk0", [128, 128]), b.ps("sc_pk1", [128, 128])]
    st["PQ"] = b.ps("sc_pq", [128, 128])
    st["PSB"] = b.ps("sc_psb", [128, 512])
    st["PT"] = b.ps("sc_pt", [128, 1024], BF16)
    st["PGR"] = b.ps("sc_pgr", [128, 512])
    st["PTE"] = st["PGR"][:, 384:392]
    return st


def emit_scan_setup(b, c0, st):
    rowi, coli = c0["rowi"], c0["coli"]
    rblk = b.sb("sc_rblk", [128, 128], F32)
    cblk = b.sb("sc_cblk", [128, 128], F32)
    mi1 = b.sb("sc_mi1", [128, 128], F32)
    mi2 = b.sb("sc_mi2", [128, 128], F32)
    iblk = b.sb("sc_iblk", [128, 128], mybir.dt.int32)
    iblk2 = b.sb("sc_iblk2", [128, 128], mybir.dt.int32)
    t0 = c0["iota_tok"]

    def fsetup(g):
        g.memset(st["ones128"][:], 1.0 / 128.0)
        g.memset(st["epsa"][:], EPS_ATTN)
        g.memset(st["S_f32"][:], 0.0)
        g.memset(st["S_sb"][:], 0.0)
        g.memset(st["vnfm"][:], 0.0)
        return g.memset(st["vntm"][:], 0.0)
    tz = b.op("g", fsetup)
    t1 = b.op("v", lambda v: v.tensor_single_scalar(out=iblk[:], in_=rowi[:], scalar=4,
                                                    op=ALU.arith_shift_right), deps=[t0])
    t2 = b.op("v", lambda v: v.tensor_single_scalar(out=iblk2[:], in_=coli[:], scalar=4,
                                                    op=ALU.arith_shift_right), deps=[t1])
    t3 = b.op("v", lambda v: v.tensor_tensor(out=mi1[:], in0=iblk2[:], in1=iblk[:],
                                             op=ALU.is_gt), deps=[t2])
    t4 = b.op("v", lambda v: v.tensor_tensor(out=mi2[:], in0=iblk2[:], in1=iblk[:],
                                             op=ALU.is_equal), deps=[t2])
    t5 = b.op("v", lambda v: v.tensor_tensor(out=rblk[:], in0=coli[:], in1=rowi[:],
                                             op=ALU.is_ge), deps=[t2])
    t6 = b.op("v", lambda v: v.tensor_tensor(out=mi2[:], in0=mi2[:], in1=rblk[:],
                                             op=ALU.mult), deps=[t4, t5])
    t7 = b.op("v", lambda v: v.tensor_tensor(out=mi1[:], in0=mi1[:], in1=mi2[:],
                                             op=ALU.add), deps=[t3, t6])
    t8 = b.op("v", lambda v: v.tensor_copy(out=st["blockmask"][:], in_=mi1[:]),
              deps=[t7])
    t9 = b.op("v", lambda v: v.tensor_tensor(out=st["w2"][:], in0=st["lnw"][:, 0:8],
                                             in1=st["lnw"][:, 0:8], op=ALU.mult),
              deps=[st["lnw_ld"]])
    st["mask_done"] = t8
    st["w2_done"] = t9
    st["S_init"] = tz
    st["Sf_init"] = tz
    st["vnfm_free"] = tz
    st["vntm_w"] = tz
    return [t8, t9, tz]


def emit_scan_macro(b, c0, st, qcols, kcols, vbcols, gcols, etacol_ap, ogcols,
                    dep_proj):
    H = 8
    S_sb, S_f32 = st["S_sb"], st["S_f32"]
    GT, AQT = st["GT"], st["AQT"]
    keitm, vntm, vnfm = st["keitm"], st["vntm"], st["vnfm"]
    PSB, PT, PGR = st["PSB"], st["PT"], st["PGR"]
    khb, sqb, khat = st["khb"], st["sqb"], st["khat"]
    gwb, gkb = st["gwb"], st["gkb"]
    f1, f2, f3 = st["f1"], st["f2"], st["f3"]
    w_bc = st["lnw"][:, 0:8].unsqueeze(2).to_broadcast((128, 8, 16))
    w2_bc = st["w2"][:, 0:8].unsqueeze(2).to_broadcast((128, 8, 16))
    nw_bc = st["nw"][:, 0:8].unsqueeze(2).to_broadcast((128, 8, 16))
    h3 = lambda ap: ap.rearrange("p (h t) -> p h t", h=8)

    t_etr = b.op("t", lambda t: t.transpose(st["PTE"], etacol_ap,
                                            c0["identf"][0:8, 0:8]),
                 deps=[dep_proj, st.get("pgr_free")])
    t_eta = b.op("v", lambda v: v.tensor_copy(out=st["eta_tm"][:, 0:8], in_=st["PTE"]),
                 deps=[t_etr])
    t_neta = b.op("v", lambda v: v.tensor_single_scalar(out=st["neta"][:, 0:8],
                                                        in_=st["eta_tm"][:, 0:8],
                                                        scalar=-1.0, op=ALU.mult),
                  deps=[t_eta])

    gram_toks = []
    prev_gr = t_neta
    for h in range(H):
        def fgr(t, _h=h):
            t.matmul(PGR[:, 0:128], kcols(_h), kcols(_h), start=True, stop=True)
            return t.matmul(PGR[:, 128:256], kcols(_h), qcols(_h), start=True, stop=True)
        t_g = b.op("t", fgr, deps=[dep_proj, prev_gr, st.get("gram_sbuf_free") if h == 0 else None])
        t_gte = b.op("v", lambda v, _h=h: v.tensor_scalar(
            out=GT[:, _h * 128:(_h + 1) * 128], in0=PGR[:, 0:128],
            scalar1=st["neta"][:, _h:_h + 1], scalar2=None, op0=ALU.mult),
            deps=[t_g])
        t_aqe = b.op("v", lambda v, _h=h: v.scalar_tensor_tensor(
            out=AQT[:, _h * 128:(_h + 1) * 128], in0=PGR[:, 128:256],
            scalar=st["neta"][:, _h:_h + 1], in1=st["blockmask"][:],
            op0=ALU.mult, op1=ALU.mult),
            deps=[t_g, st["mask_done"]])
        prev_gr = t_aqe
        gram_toks += [t_gte, t_aqe]
    gram_done = gram_toks[-1]

    kei_toks = []
    prev_pt = st.get("pt_free")
    for h in range(H):
        t_kt = b.op("t", lambda t, _h=h: t.transpose(PT[:, 0:128], kcols(_h),
                                                     c0["ident"][:]),
                    deps=[dep_proj, prev_pt, st.get("keitm_free") if h == 0 else None])
        t_ke = b.op("v", lambda v, _h=h: v.tensor_scalar(
            out=keitm[:, _h * 128:(_h + 1) * 128], in0=PT[:, 0:128],
            scalar1=st["neta"][:, _h:_h + 1], scalar2=None, op0=ALU.mult),
            deps=[t_kt, t_neta])
        prev_pt = t_ke
        kei_toks.append(t_ke)
    st["pt_free"] = prev_pt

    o_toks = []
    last_tr = None
    for c in range(8):
        pk = st["PK"][c % 2][:]
        pq = st["PQ"][:]

        def fbase(t, _c=c, _pk=pk):
            for h in range(H):
                tt = t.matmul(_pk[:, h * 16:(h + 1) * 16],
                              S_sb[:, h * 128:(h + 1) * 128],
                              kcols(h)[:, _c * 16:(_c + 1) * 16], start=(h == 0),
                              stop=(_c == 0 and h == H - 1))
            if _c > 0:
                for h in range(H):
                    tt = t.matmul(_pk[:, h * 16:(h + 1) * 16],
                                  vntm[0:16 * _c, h * 128:(h + 1) * 128],
                                  GT[0:16 * _c, h * 128 + 16 * _c:h * 128 + 16 * _c + 16],
                                  start=False, stop=(h == H - 1))
            return tt
        dep_b = [dep_proj, st["S_init"], st.get(f"pk{c % 2}_free"), st["vntm_w"],
                 gram_done]
        t_base = b.op("t", fbase, deps=dep_b)

        def fbq(t, _c=c, _pq=pq):
            for h in range(H):
                tt = t.matmul(_pq[:, h * 16:(h + 1) * 16],
                              S_sb[:, h * 128:(h + 1) * 128],
                              qcols(h)[:, _c * 16:(_c + 1) * 16], start=(h == 0),
                              stop=False)
            return tt
        t_bq = b.op("t", fbq, deps=[dep_proj, st["S_init"], st.get("pq_free"),
                                    gram_done])

        t_khb = b.op("v", lambda v, _pk=pk: v.tensor_copy(out=khb[:], in_=_pk),
                     deps=[t_base])
        t_mu = b.op("t", lambda t: t.matmul(PSB[:, 0:128], st["ones128"][:], khb[:],
                                            start=True, stop=True),
                    deps=[t_khb, st.get("psb_free")])
        t_muc = b.op("v", lambda v: v.tensor_copy(out=st["murd"][:], in_=PSB[:, 0:128]),
                     deps=[t_mu])
        t_khc = b.op("v", lambda v, _pk=pk: v.tensor_tensor(out=f3[:], in0=_pk,
                                                            in1=st["murd"][:],
                                                            op=ALU.subtract),
                     deps=[t_muc])
        t_sq = b.op("s", lambda s: s.activation(out=sqb[:], in_=f3[:], func=AF.Square),
                    deps=[t_khc])
        t_e2 = b.op("t", lambda t: t.matmul(PSB[:, 128:256], st["ones128"][:], sqb[:],
                                            start=True, stop=True), deps=[t_sq])
        t_C = b.op("s", lambda s: s.activation(out=f1[:], in_=PSB[:, 128:256],
                                               func=AF.Sqrt, bias=st["epsa"][:, 0:1]),
                   deps=[t_e2])
        t_rs = b.op("v", lambda v: v.reciprocal(out=st["rstd"][:], in_=f1[:]),
                    deps=[t_C])
        t_h2 = b.op("v", lambda v: v.tensor_tensor(out=khat[:], in0=f3[:],
                                                   in1=st["rstd"][:], op=ALU.mult),
                    deps=[t_rs])
        t_h1 = t_h2
        t_tb = b.op("g", lambda g, _c=c: g.tensor_tensor(
            out=h3(st["tb"][:]), in0=vbcols(None)[:, :, _c * 16:(_c + 1) * 16],
            in1=kcols(None)[:, :, _c * 16:(_c + 1) * 16], op=ALU.add),
            deps=[dep_proj])
        t_t2 = b.op("g", lambda g: g.tensor_tensor(
            out=h3(st["t2"][:]), in0=h3(st["tb"][:]), in1=w_bc, op=ALU.mult),
            deps=[t_tb, st["w2_done"]])
        t_u = b.op("v", lambda v: v.tensor_tensor(
            out=h3(f2[:]), in0=h3(khat[:]), in1=w2_bc, op=ALU.mult),
            deps=[t_h2, st["w2_done"]])
        t_gw = b.op("v", lambda v: v.tensor_tensor(out=f3[:], in0=f2[:],
                                                   in1=st["t2"][:], op=ALU.add),
                    deps=[t_u, t_t2])
        t_gwb = b.op("v", lambda v: v.tensor_copy(out=gwb[:], in_=f3[:]), deps=[t_gw])
        t_gk = b.op("v", lambda v: v.tensor_tensor(out=gkb[:], in0=f3[:], in1=khat[:],
                                                   op=ALU.mult), deps=[t_gw])

        def fmg(t):
            t.matmul(PSB[:, 256:384], st["ones128"][:], gwb[:], start=True, stop=True)
            return t.matmul(PSB[:, 384:512], st["ones128"][:], gkb[:],
                            start=True, stop=True)
        t_mg = b.op("t", fmg, deps=[t_gk, t_gwb, t_muc])
        t_v1 = b.op("v", lambda v: v.tensor_tensor(out=f2[:], in0=khat[:],
                                                   in1=PSB[:, 384:512], op=ALU.mult),
                    deps=[t_mg])
        t_v2 = b.op("v", lambda v: v.tensor_tensor(out=f1[:], in0=f3[:],
                                                   in1=PSB[:, 256:384], op=ALU.subtract),
                    deps=[t_mg])
        t_v3 = b.op("v", lambda v: v.tensor_tensor(out=f1[:], in0=f1[:], in1=f2[:],
                                                   op=ALU.subtract), deps=[t_v1, t_v2])
        t_vn = b.op("v", lambda v, _c=c: v.tensor_tensor(
            out=h3(vnfm[:])[:, :, _c * 16:(_c + 1) * 16],
            in0=h3(f1[:]), in1=h3(st["rstd"][:]), op=ALU.mult),
            deps=[t_v3, st["vnfm_free"] if c == 0 else None])

        def ftr(t, _c=c):
            for h in range(H):
                tt = t.transpose(PT[0:16 * (_c + 1), h * 128:(h + 1) * 128],
                                 vnfm[:, h * 128:h * 128 + 16 * (_c + 1)],
                                 c0["ident"][:])
            return tt
        t_tr = b.op("t", ftr, deps=[t_vn, st.get("pt_free")])
        t_ev = b.op("v", lambda v, _c=c: v.tensor_copy(
            out=vntm[0:16 * (_c + 1), :], in_=PT[0:16 * (_c + 1), :]), deps=[t_tr])
        st["pt_free"] = t_ev
        st["vntm_w"] = t_ev
        last_tr = t_tr

        def fqc(t, _c=c, _pq=pq):
            for h in range(H):
                tt = t.matmul(_pq[:, h * 16:(h + 1) * 16],
                              vntm[0:16 * (_c + 1), h * 128:(h + 1) * 128],
                              AQT[0:16 * (_c + 1),
                                  h * 128 + 16 * _c:h * 128 + 16 * _c + 16],
                              start=False, stop=(h == H - 1))
            return tt
        t_qc = b.op("t", fqc, deps=[t_ev, gram_done, t_bq])
        st["gram_sbuf_free"] = t_qc

        t_qhb = b.op("v", lambda v, _pq=pq: v.tensor_copy(out=khb[:], in_=_pq),
                     deps=[t_qc])
        t_mu2 = b.op("t", lambda t: t.matmul(PSB[:, 0:128], st["ones128"][:], khb[:],
                                             start=True, stop=True),
                     deps=[t_qhb, t_v3, t_muc])
        t_muc2 = b.op("v", lambda v: v.tensor_copy(out=st["murd"][:], in_=PSB[:, 0:128]),
                      deps=[t_mu2])
        t_qhc = b.op("v", lambda v, _pq=pq: v.tensor_tensor(out=f3[:], in0=_pq,
                                                            in1=st["murd"][:],
                                                            op=ALU.subtract),
                     deps=[t_muc2])
        t_sq2 = b.op("s", lambda s: s.activation(out=sqb[:], in_=f3[:],
                                                 func=AF.Square), deps=[t_qhc])
        t_e22 = b.op("t", lambda t: t.matmul(PSB[:, 128:256], st["ones128"][:], sqb[:],
                                             start=True, stop=True), deps=[t_sq2])
        t_C2 = b.op("s", lambda s: s.activation(out=f1[:], in_=PSB[:, 128:256],
                                                func=AF.Sqrt, bias=st["epsa"][:, 0:1]),
                    deps=[t_e22])
        t_rs2 = b.op("v", lambda v: v.reciprocal(out=st["rstd"][:], in_=f1[:]),
                     deps=[t_C2])
        t_o1 = b.op("v", lambda v: v.tensor_tensor(out=f3[:], in0=f3[:],
                                                   in1=st["rstd"][:],
                                                   op=ALU.mult), deps=[t_rs2])
        t_o3 = b.op("g", lambda g: g.tensor_tensor(
            out=h3(f3[:]), in0=h3(f3[:]), in1=w_bc, op=ALU.mult), deps=[t_o1])
        lnb_bc = st["lnb"][:, 0:8].unsqueeze(2).to_broadcast((128, 8, 16))
        t_o3b = b.op("v", lambda v: v.tensor_tensor(
            out=h3(f3[:]), in0=h3(f3[:]), in1=lnb_bc, op=ALU.add),
            deps=[t_o3, st["lnb_done"]])
        t_o4 = b.op("v", lambda v, _c=c: v.tensor_tensor(
            out=h3(f3[:]), in0=h3(f3[:]),
            in1=qcols(None)[:, :, _c * 16:(_c + 1) * 16], op=ALU.add), deps=[t_o3b])
        t_o5 = b.op("v", lambda v, _c=c: v.tensor_tensor(
            out=h3(f3[:]), in0=h3(f3[:]),
            in1=gcols(None)[:, :, _c * 16:(_c + 1) * 16], op=ALU.mult), deps=[t_o4])
        t_sqo = b.op("s", lambda s: s.activation(out=sqb[:], in_=f3[:], func=AF.Square),
                     deps=[t_o5])
        t_ms = b.op("t", lambda t: t.matmul(PSB[:, 256:384], st["ones128"][:], sqb[:],
                                            start=True, stop=True),
                    deps=[t_sqo, t_v1, t_v2])
        t_c4 = b.op("s", lambda s: s.activation(out=f1[:], in_=PSB[:, 256:384],
                                                func=AF.Sqrt, bias=st["epsa"][:, 0:1]),
                    deps=[t_ms])
        t_r4 = b.op("v", lambda v: v.reciprocal(out=f2[:], in_=f1[:]), deps=[t_c4])
        t_o6 = b.op("v", lambda v: v.tensor_tensor(out=f3[:], in0=f3[:], in1=f2[:],
                                                   op=ALU.mult), deps=[t_r4])
        t_og = b.op("v", lambda v, _c=c: v.tensor_tensor(
            out=ogcols(None)[:, :, _c * 16:(_c + 1) * 16],
            in0=h3(f3[:]), in1=nw_bc, op=ALU.mult),
            deps=[t_o6, st["nw_done"], st.get("og_free")])
        o_toks.append(t_og)
        st[f"pk{c % 2}_free"] = t_h1
        st["pq_free"] = t_qhc
        st["psb_free"] = t_r4
    st["vnfm_free"] = last_tr

    prev = st.get("pgr_free")
    s_toks = []
    last_ds = None
    for h in range(H):
        def fds(t, _h=h):
            return t.matmul(PGR[:, 256:384], keitm[:, _h * 128:(_h + 1) * 128],
                            vntm[:, _h * 128:(_h + 1) * 128], start=True, stop=True)
        deps_ds = [kei_toks[h], st["vntm_w"], prev]
        if h == 0:
            deps_ds += [o_toks[-1], t_neta]
        t_ds = b.op("t", fds, deps=deps_ds)
        last_ds = t_ds
        t_sa = b.op("v", lambda v, _h=h: v.tensor_tensor(
            out=S_f32[:, _h * 128:(_h + 1) * 128],
            in0=S_f32[:, _h * 128:(_h + 1) * 128], in1=PGR[:, 256:384], op=ALU.add),
            deps=[t_ds, st["Sf_init"]])
        t_sc = b.op("s", lambda s, _h=h: s.copy(out=S_sb[:, _h * 128:(_h + 1) * 128],
                                                in_=S_f32[:, _h * 128:(_h + 1) * 128]),
                    deps=[t_sa])
        prev = t_sa
        s_toks.append(t_sc)
    st["pgr_free"] = prev
    st["keitm_free"] = last_ds
    st["S_init"] = s_toks[-1]
    st["Sf_init"] = s_toks[-1]
    return o_toks, s_toks


def build_attn_launch(ntok=8192, debug=False):
    b = B()
    TP = 256
    ntile = ntok // TP
    H = 8

    x_d = b.din("x1", [KC, 128, ntok], BF16)
    wq_d = b.din("wqT", [D, 1024], F32)
    wk_d = b.din("wkT", [D, 1024], F32)
    wv_d = b.din("wvT", [D, 1024], F32)
    wb_d = b.din("wbT", [D, 8], F32)
    wg1_d = b.din("wg1T", [D, 128], F32)
    wg2_d = b.din("wg2T", [128, 1024], F32)
    lnw_d = b.din("lnw", [128, 8], F32)
    lnb_d = b.din("lnb", [128, 8], F32)
    nw_d = b.din("nw", [128, 8], F32)
    og_d = b.dout("og", [H, 128, ntok], BF16)
    if debug:
        dq = b.dout("dbg_q", [128, H * 256], BF16)
        dk = b.dout("dbg_k", [128, H * 256], BF16)
        dv = b.dout("dbg_v", [128, H * 256], BF16)
        dg = b.dout("dbg_g", [128, H * 256], BF16)
        de = b.dout("dbg_e", [8, 256], F32)

    c0 = emit_consts(b)
    st = scan_state(b)
    st["lnw"] = b.sb("lnw_s", [128, 8], F32)
    st["lnb"] = b.sb("lnb_s", [128, 8], F32)
    st["nw"] = b.sb("nw_s", [128, 8], F32)
    st["onesd128"] = b.sb("a_onesd128", [128, 128], BF16)
    st["onesf128"] = b.sb("a_onesf128", [128, 128], BF16)
    b.epsln2 = b.sb("a_epsln2", [128, 1], F32)
    b.op("g", lambda g: (g.memset(st["onesd128"][:], 1.0 / 2048.0),
                         g.memset(st["onesf128"][:], 1.0),
                         g.memset(b.epsln2[:], EPS_LN))[-1])

    wq = b.sb("wq", [128, KC * 1024], BF16)
    wk = b.sb("wk", [128, KC * 1024], BF16)
    wv = b.sb("wv", [128, KC * 1024], BF16)
    wbe = b.sb("wbe", [128, KC * 8], BF16)
    wg1 = b.sb("wg1", [128, KC * 128], BF16)
    wg2 = b.sb("wg2", [128, 1024], BF16)
    nqs = b.sb("nqs", [1, 1024], BF16)
    nks = b.sb("nks", [1, 1024], BF16)
    nvs = b.sb("nvs", [1, 1024], BF16)
    nbs2 = b.sb("nbs2", [1, 8], BF16)
    ngs = b.sb("ngs", [1, 128], BF16)

    xt = [b.sb(f"a_xt{i}", [128, KC * TP], BF16) for i in range(2)]
    qf = [b.sb(f"a_q{i}", [128, H * TP], BF16) for i in range(2)]
    kf = [b.sb(f"a_k{i}", [128, H * TP], BF16) for i in range(2)]
    vf = [b.sb(f"a_v{i}", [128, H * TP], BF16) for i in range(2)]
    gf = [b.sb(f"a_g{i}", [128, H * TP], BF16) for i in range(2)]
    ef = [b.sb(f"a_eta{i}", [8, TP], F32) for i in range(2)]
    ogb = [b.sb(f"a_og{i}", [128, H * TP], BF16) for i in range(2)]
    gm = b.sb("a_gm", [128, TP], BF16)
    sq2 = b.sb("a_sq", [128, KC * TP], BF16)
    kln = b.sb("a_kln", [128, TP], F32)
    e1s = b.sb("a_e1", [128, TP], F32)
    rstdbc = b.sb("a_rstd", [128, TP], F32)
    murow = b.sb("a_mu", [1, TP], BF16)
    rows = b.sb("a_rows", [128, TP], F32)

    pr = [b.ps("a_pr0", [128, 512]), b.ps("a_pr1", [128, 512])]
    PGR = st["PGR"]

    lds = []
    for dst, src, nk in ((wq, wq_d, KC), (wk, wk_d, KC), (wv, wv_d, KC),
                         (wbe, wb_d, KC), (wg1, wg1_d, KC)):
        ncols = src.shape[1]
        lds.append(b.dma("g", lambda g, _d=dst, _s=src, _n=ncols: g.dma_start(
            out=_d[:].rearrange("p (k o) -> p k o", k=KC),
            in_=_s.rearrange("(k p) o -> p k o", p=128))))
    lds.append(b.dma("g", lambda g: g.dma_start(out=wg2[:], in_=wg2_d)))
    t_lnw = b.dma("y", lambda y: y.dma_start(out=st["lnw"][:], in_=lnw_d))
    t_lnb = b.dma("y", lambda y: y.dma_start(out=st["lnb"][:], in_=lnb_d))
    t_nw = b.dma("y", lambda y: y.dma_start(out=st["nw"][:], in_=nw_d))
    st["lnw_ld"] = t_lnw
    st["lnb_done"] = t_lnb
    st["nw_done"] = t_nw
    emit_scan_setup(b, c0, st)

    prevsum = None
    sums = [(wq, nqs, 1024), (wk, nks, 1024), (wv, nvs, 1024), (wbe, nbs2, 8),
            (wg1, ngs, 128)]
    for wsb, dst, ncols in sums:
        for o in range(0, ncols, 512):
            w_ = min(512, ncols - o)
            def fsum(t, _w=wsb, _o=o, _ww=w_, _nc=ncols):
                for k in range(KC):
                    tt = t.matmul(PGR[0:1, 0:_ww], c0["negones"][:, 0:1],
                                  _w[:, k * _nc + _o:k * _nc + _o + _ww],
                                  start=(k == 0), stop=(k == KC - 1))
                return tt
            tm = b.op("t", fsum, deps=lds + [prevsum])
            prevsum = b.op("v", lambda v, _d=dst, _o=o, _ww=w_: v.tensor_copy(
                out=_d[0:1, _o:_o + _ww], in_=PGR[0:1, 0:_ww]), deps=[tm])
    st["pgr_free"] = prevsum

    free_x = [None, None]
    free_proj = [None, None]
    free_og = [None, None]
    last_store = None
    for p in range(ntile):
        pp = p % 2
        t_ld = b.dma("g", lambda g, _p=p, _q=pp: g.dma_start(
            out=xt[_q][:].rearrange("p (k t) -> p k t", k=KC),
            in_=x_d[:, :, _p * TP:(_p + 1) * TP].rearrange("k p t -> p k t")),
            deps=[free_x[pp]] + ([prevsum] if p < 2 else []))

        def fsq(v, _q=pp):
            for k in range(KC):
                tt = v.tensor_tensor(out=sq2[:, k * TP:(k + 1) * TP],
                                     in0=xt[_q][:, k * TP:(k + 1) * TP],
                                     in1=xt[_q][:, k * TP:(k + 1) * TP], op=ALU.mult)
            return tt
        t_sq = b.op("v", fsq, deps=[t_ld])

        def fst(t, _q=pp):
            for k in range(KC):
                t.matmul(PGR[:, 0:TP], st["onesd128"][:], xt[_q][:, k * TP:(k + 1) * TP],
                         start=(k == 0), stop=(k == KC - 1))
            for k in range(KC):
                tt = t.matmul(PGR[:, TP:2 * TP], st["onesd128"][:],
                              sq2[:, k * TP:(k + 1) * TP],
                              start=(k == 0), stop=(k == KC - 1))
            return tt
        t_st = b.op("t", fst, deps=[t_sq, st.get("pgr_free")])
        t_A = b.op("s", lambda s: s.activation(out=rows[:], in_=PGR[:, 0:TP],
                                               func=AF.Square), deps=[t_st])
        t_B = b.op("v", lambda v: v.tensor_tensor(out=e1s[:], in0=PGR[:, TP:2 * TP],
                                                  in1=rows[:], op=ALU.subtract),
                   deps=[t_A])
        t_C = b.op("s", lambda s: s.activation(out=rows[:], in_=e1s[:], func=AF.Sqrt,
                                               bias=b.epsln2[:, 0:1]), deps=[t_B])
        t_rs = b.op("v", lambda v: v.reciprocal(out=rstdbc[:], in_=rows[:]),
                    deps=[t_C])
        t_mur = b.op("v", lambda v: v.tensor_copy(out=murow[0:1, 0:TP],
                                                  in_=PGR[0:1, 0:TP]), deps=[t_st])
        st["pgr_free"] = t_mur

        evac_toks = []
        prev_pr = [free_proj[pp], st.get("pr1_free")]

        def projmm(wsb, ncols, fc, bank, nsum, _q=pp):
            def f(t, _w=wsb, _n=ncols, _fc=fc, _b=bank, _qq=_q, _ns=nsum):
                for k in range(KC):
                    t.matmul(pr[_b][:, 0:TP],
                             _w[:, k * _n + _fc * 128:k * _n + _fc * 128 + 128],
                             xt[_qq][:, k * TP:(k + 1) * TP],
                             start=(k == 0), stop=False)
                return t.matmul(pr[_b][:, 0:TP], _ns[0:1, _fc * 128:_fc * 128 + 128],
                                murow[0:1, 0:TP], start=False, stop=True)
            return f

        bank = 0
        for h in range(H):
            t_mm = b.op("t", projmm(wq, 1024, h, bank, nqs),
                        deps=[t_ld, t_mur, prev_pr[bank]])
            t_e1 = b.op("v", lambda v, _b=bank: v.tensor_tensor(
                out=e1s[:], in0=pr[_b][:, 0:TP], in1=rstdbc[:], op=ALU.mult),
                deps=[t_mm, t_rs, evac_toks[-1] if evac_toks else None])
            t_e2 = b.op("s", lambda s, _h=h, _q=pp: s.activation(
                out=qf[_q][:, _h * TP:(_h + 1) * TP], in_=e1s[:], func=AF.Silu),
                deps=[t_e1])
            prev_pr[bank] = t_e2
            evac_toks.append(t_e2)
            bank ^= 1
        for h in range(H):
            t_mm = b.op("t", projmm(wk, 1024, h, bank, nks),
                        deps=[t_ld, t_mur, prev_pr[bank]])
            t_e1 = b.op("v", lambda v, _b=bank: v.tensor_tensor(
                out=kln[:], in0=pr[_b][:, 0:TP], in1=rstdbc[:], op=ALU.mult),
                deps=[t_mm, t_rs, evac_toks[-1] if evac_toks else None])
            prev_pr[bank] = t_e1
            t_ks = b.op("s", lambda s: s.activation(out=sq2[:, 0:TP], in_=kln[:],
                                                    func=AF.Square), deps=[t_e1])
            t_km = b.op("t", lambda t: t.matmul(PGR[:, 0:TP], st["onesf128"][:],
                                                sq2[:, 0:TP], start=True, stop=True),
                        deps=[t_ks, st.get("pgr_free")])
            t_kc = b.op("s", lambda s: s.activation(out=rows[:], in_=PGR[:, 0:TP],
                                                    func=AF.Sqrt,
                                                    bias=st["epsa"][:, 0:1]),
                        deps=[t_km])
            st["pgr_free"] = t_kc
            t_kr = b.op("v", lambda v: v.reciprocal(out=e1s[:], in_=rows[:]),
                        deps=[t_kc])
            t_kf = b.op("v", lambda v, _h=h, _q=pp: v.tensor_tensor(
                out=kf[_q][:, _h * TP:(_h + 1) * TP], in0=kln[:], in1=e1s[:],
                op=ALU.mult), deps=[t_kr])
            evac_toks.append(t_kf)
            bank ^= 1
        for h in range(H):
            t_mm = b.op("t", projmm(wv, 1024, h, bank, nvs),
                        deps=[t_ld, t_mur, prev_pr[bank]])
            t_e1 = b.op("v", lambda v, _b=bank: v.tensor_tensor(
                out=e1s[:], in0=pr[_b][:, 0:TP], in1=rstdbc[:], op=ALU.mult),
                deps=[t_mm, t_rs, evac_toks[-1] if evac_toks else None])
            t_vb = b.op("v", lambda v, _h=h, _q=pp: v.tensor_tensor(
                out=vf[_q][:, _h * TP:(_h + 1) * TP],
                in0=st["lnb"][:, _h:_h + 1].to_broadcast((128, TP)),
                in1=e1s[:], op=ALU.subtract), deps=[t_e1, t_lnb])
            prev_pr[bank] = t_vb
            evac_toks.append(t_vb)
            bank ^= 1
        def feta(t, _b=bank, _q=pp):
            for k in range(KC):
                t.matmul(pr[_b][0:8, 0:TP], wbe[:, k * 8:(k + 1) * 8],
                         xt[_q][:, k * TP:(k + 1) * TP], start=(k == 0), stop=False)
            return t.matmul(pr[_b][0:8, 0:TP], nbs2[0:1, 0:8], murow[0:1, 0:TP],
                            start=False, stop=True)
        t_mm = b.op("t", feta, deps=[t_ld, t_mur, prev_pr[bank]])
        t_ee = b.op("v", lambda v, _b=bank, _q=pp: v.tensor_tensor(
            out=ef[_q][0:8, 0:TP], in0=pr[_b][0:8, 0:TP], in1=rstdbc[0:8, 0:TP],
            op=ALU.mult), deps=[t_mm, t_rs, evac_toks[-1]])
        t_es = b.op("s", lambda s, _q=pp: s.activation(out=ef[_q][0:8, 0:TP],
                                                       in_=ef[_q][0:8, 0:TP],
                                                       func=AF.Sigmoid), deps=[t_ee])
        t_e2x = b.op("v", lambda v, _q=pp: v.tensor_single_scalar(
            out=ef[_q][0:8, 0:TP], in_=ef[_q][0:8, 0:TP],
            scalar=2.0, op=ALU.mult), deps=[t_es])
        prev_pr[bank] = t_ee
        evac_toks.append(t_e2x)
        bank ^= 1
        def fgm(t, _b=bank, _q=pp):
            for k in range(KC):
                t.matmul(pr[_b][:, 0:TP], wg1[:, k * 128:(k + 1) * 128],
                         xt[_q][:, k * TP:(k + 1) * TP], start=(k == 0), stop=False)
            return t.matmul(pr[_b][:, 0:TP], ngs[0:1, 0:128], murow[0:1, 0:TP],
                            start=False, stop=True)
        t_mm = b.op("t", fgm, deps=[t_ld, t_mur, prev_pr[bank]])
        t_gm = b.op("v", lambda v, _b=bank: v.tensor_copy(out=gm[:], in_=pr[_b][:, 0:TP]),
                    deps=[t_mm, evac_toks[-1]])
        prev_pr[bank] = t_gm
        bank ^= 1
        for h in range(H):
            def fg2(t, _h=h, _b=bank):
                return t.matmul(pr[_b][:, 0:TP], wg2[:, _h * 128:(_h + 1) * 128],
                                gm[:], start=True, stop=True)
            t_mm = b.op("t", fg2, deps=[t_gm, prev_pr[bank]])
            t_e1 = b.op("v", lambda v, _b=bank: v.tensor_tensor(
                out=e1s[:], in0=pr[_b][:, 0:TP], in1=rstdbc[:], op=ALU.mult),
                deps=[t_mm, t_rs, evac_toks[-1]])
            t_g2 = b.op("s", lambda s, _h=h, _q=pp: s.activation(
                out=gf[_q][:, _h * TP:(_h + 1) * TP], in_=e1s[:], func=AF.Sigmoid),
                deps=[t_e1])
            prev_pr[bank] = t_g2
            evac_toks.append(t_g2)
            bank ^= 1
        st["pr1_free"] = prev_pr[1]
        free_proj_tok = prev_pr[0]
        dep_proj = evac_toks[-1]

        def mkcols(buf, m):
            def f(h):
                if h is None:
                    return buf[:].rearrange("p (h t) -> p h t", h=H)[
                        :, :, m * 128:(m + 1) * 128]
                return buf[:, h * TP + m * 128:h * TP + (m + 1) * 128]
            return f
        if debug and p == 0:
            for dd, ss_ in ((dq, qf[0]), (dk, kf[0]), (dv, vf[0]), (dg, gf[0]),
                            (de, ef[0])):
                b.dma("y", lambda y, _d=dd, _s=ss_: y.dma_start(out=_d, in_=_s[:]),
                      deps=[dep_proj])
        st["og_free"] = free_og[pp]
        all_o = []
        for m in range(2):
            o_toks, s_toks = emit_scan_macro(
                b, c0, st, mkcols(qf[pp], m), mkcols(kf[pp], m), mkcols(vf[pp], m),
                mkcols(gf[pp], m), ef[pp][0:8, m * 128:(m + 1) * 128],
                mkcols(ogb[pp], m), dep_proj=dep_proj)
            all_o += o_toks
        free_proj[pp] = all_o[-1]
        free_x[pp] = t_gm
        t_store = b.dma("y", lambda y, _p=p, _q=pp: y.dma_start(
            out=og_d[:, :, _p * TP:(_p + 1) * TP].rearrange("h p t -> p h t"),
            in_=ogb[_q][:].rearrange("p (h t) -> p h t", h=H)), deps=all_o)
        free_og[pp] = t_store
        last_store = t_store
    return b.finalize()


def _fm(x):
    return np.ascontiguousarray(x.T.reshape(KC, 128, -1))


def kernel(L0_attn_x, L0_attn_Wq, L0_attn_Wk, L0_attn_Wv, L0_attn_Wo,
           L0_attn_Wbeta, L0_attn_ln_w, L0_attn_ln_b, L0_attn_Wg1, L0_attn_Wg2,
           L0_attn_norm_w, L0_ffn_base_w, L0_ffn_base_b, L0_ffn_router_w,
           L0_ffn_router_b, L0_ffn_lora_A, L0_ffn_lora_B):
    f32 = np.float32
    x = np.asarray(L0_attn_x, f32)
    Bb, T, _ = x.shape
    HT = T // 2

    ffn_w = {
        "bwT": np.ascontiguousarray(np.asarray(L0_ffn_base_w, f32).T),
        "rwT": np.ascontiguousarray(np.asarray(L0_ffn_router_w, f32).T),
        "laf": np.ascontiguousarray(np.asarray(L0_ffn_lora_A, f32)
                                    .transpose(1, 0, 2).reshape(D, 128)),
        "lbf": np.ascontiguousarray(np.asarray(L0_ffn_lora_B, f32).reshape(128, D)),
        "bbt": np.ascontiguousarray(np.asarray(L0_ffn_base_b, f32)
                                    .reshape(KC, 128).T),
        "rbv": np.ascontiguousarray(np.asarray(L0_ffn_router_b, f32).reshape(1, 8)),
    }

    nc1 = build_ffn_launch(HT, with_wo=False, in_f32=True, out_f32=False)
    maps1 = []
    for c in range(NCORE):
        bi, hg = c // 2, c % 2
        m = {"xin": _fm(x[bi, hg * HT:(hg + 1) * HT])}
        m.update(ffn_w)
        maps1.append(m)
    r1 = run_bass_kernel_spmd(nc1, maps1, core_ids=list(range(NCORE)))
    x1 = [r1.results[c]["xout"] for c in range(NCORE)]

    Wq = np.asarray(L0_attn_Wq, f32)
    Wk = np.asarray(L0_attn_Wk, f32)
    Wv = np.asarray(L0_attn_Wv, f32)
    Wbe = np.asarray(L0_attn_Wbeta, f32)
    Wg1 = np.ascontiguousarray(np.asarray(L0_attn_Wg1, f32))
    Wg2 = np.asarray(L0_attn_Wg2, f32)
    lnw = np.asarray(L0_attn_ln_w, f32)
    lnb = np.asarray(L0_attn_ln_b, f32)
    nwv = np.asarray(L0_attn_norm_w, f32).reshape(KC, 128)
    nc2 = build_attn_launch(T)
    maps2 = []
    for c in range(NCORE):
        bi, hg = c // 2, c % 2
        sl = slice(hg * 1024, (hg + 1) * 1024)
        hs = slice(hg * 8, (hg + 1) * 8)
        x1full = np.concatenate([x1[2 * bi], x1[2 * bi + 1]], axis=2)
        maps2.append({
            "x1": x1full,
            "wqT": np.ascontiguousarray(Wq[:, sl]),
            "wkT": np.ascontiguousarray(Wk[:, sl]),
            "wvT": np.ascontiguousarray(Wv[:, sl]),
            "wbT": np.ascontiguousarray(Wbe[:, hs]),
            "wg1T": Wg1,
            "wg2T": np.ascontiguousarray(Wg2[:, sl]),
            "lnw": np.ascontiguousarray(lnw[hs].T),
            "lnb": np.ascontiguousarray(lnb[hs].T),
            "nw": np.ascontiguousarray(nwv[hs].T),
        })
    r2 = run_bass_kernel_spmd(nc2, maps2, core_ids=list(range(NCORE)))
    og = [r2.results[c]["og"] for c in range(NCORE)]

    Wo = np.ascontiguousarray(np.asarray(L0_attn_Wo, f32))
    nc3 = build_ffn_launch(HT, with_wo=True, in_f32=False, out_f32=True, TILE=256)
    maps3 = []
    for c in range(NCORE):
        bi, hg = c // 2, c % 2
        sl = slice(hg * HT, (hg + 1) * HT)
        ogfull = np.concatenate([og[2 * bi][:, :, sl], og[2 * bi + 1][:, :, sl]],
                                axis=0)
        m = {"xin": x1[c], "og": np.ascontiguousarray(ogfull), "wo": Wo}
        m.update(ffn_w)
        maps3.append(m)
    r3 = run_bass_kernel_spmd(nc3, maps3, core_ids=list(range(NCORE)))

    out = np.empty((Bb, T, D), f32)
    for c in range(NCORE):
        bi, hg = c // 2, c % 2
        out[bi, hg * HT:(hg + 1) * HT] = \
            r3.results[c]["xout"].reshape(D, HT).T
    return out
